# version 01 (final)
# baseline (speedup 1.0000x reference)
import numpy as np
import ml_dtypes
from contextlib import ExitStack

import concourse.bass as bass
import concourse.mybir as mybir
from concourse.bass_utils import run_bass_kernel_spmd

BF = ml_dtypes.bfloat16
F32 = mybir.dt.float32
BF16 = mybir.dt.bfloat16
AF = mybir.ActivationFunctionType
ALU = mybir.AluOpType
AX = mybir.AxisListType

D = 2048
TH = 2048
NTC = TH // 128
KC = D // 128
H, DK, DV, M = 16, 64, 128, 64
NHG, HPG = 4, 4
HF = 8192
NHGF = HF // 512
EPS_LN = 1e-5
EPS_RMS = float(np.finfo(np.float32).eps)

STAGE = 5
DEBUG = False
import os as _os
_V_NO_SH = _os.environ.get("V_NO_SH", "0") == "1"
_V_CLIP_ONLY = _os.environ.get("V_CLIP_ONLY", "0") == "1"
_V_REV = _os.environ.get("V_REV", "0") == "1"
_V_L = int(_os.environ.get("V_L", "9"))
_V_LN = int(_os.environ.get("V_LN", "9"))
_V_NHG = int(_os.environ.get("V_NHG", "4"))
_V_NCH = int(_os.environ.get("V_NCH", "16"))
_V_GT = _os.environ.get("V_GT", "1") == "1"
_V_SC = int(_os.environ.get("V_SC", "9"))

ENGINES = ("sync", "scalar", "vector", "tensor", "gpsimd")


class Prog:
    def __init__(self):
        self.q = {e: [] for e in ENGINES}
        self.cnt = {}

    def wait(self, eng, ev):
        if ev is not None:
            self.q[eng].append(("w", ev[0], ev[1]))

    def waits(self, eng, evs):
        best = {}
        for ev in evs:
            if ev is not None and ev[1] > best.get(ev[0], -1):
                best[ev[0]] = ev[1]
        for s, v in best.items():
            self.q[eng].append(("w", s, v))

    def do(self, eng, fn, sig=None, inc=1):
        if sig is None:
            self.q[eng].append(("i", fn, None, 0))
            return None
        self.cnt.setdefault(sig, 0)
        self.cnt[sig] += inc
        self.q[eng].append(("i", fn, sig, inc))
        return (sig, self.cnt[sig])

    def barrier(self):
        snap = [(s, v) for s, v in self.cnt.items() if v > 0]
        for e in ENGINES:
            for s, v in snap:
                self.q[e].append(("w", s, v))

    def play(self, block, sems):
        for e in ENGINES:
            items = self.q[e]

            def runner(eng, items=items):
                for it in items:
                    if it[0] == "w":
                        eng.wait_ge(sems[it[1]], it[2])
                    else:
                        ins = it[1](eng)
                        if it[2] is not None:
                            ins.then_inc(sems[it[2]], it[3])
            getattr(block, e)(runner)


class Pool:
    def __init__(self, prog, items):
        self.prog = prog
        self.slots = [[t, None] for t in items]
        self.i = 0

    def acquire(self, eng):
        s = self.slots[self.i % len(self.slots)]
        self.i += 1
        self.prog.wait(eng, s[1])
        return s


def build(debug=False, stage=None):
    if stage is None:
        stage = STAGE
    nc = bass.Bass("TRN2", target_bir_lowering=False)

    x_own = nc.dram_tensor("x_own", [TH, D], F32, kind="ExternalInput").ap()
    x_pre = nc.dram_tensor("x_pre", [TH, D], F32, kind="ExternalInput").ap()
    Wq = nc.dram_tensor("Wq", [D, H * DK], F32, kind="ExternalInput").ap()
    Wk = nc.dram_tensor("Wk", [D, H * DK], F32, kind="ExternalInput").ap()
    Wv = nc.dram_tensor("Wv", [D, H * DV], F32, kind="ExternalInput").ap()
    Wg = nc.dram_tensor("Wg", [D, H * DV], F32, kind="ExternalInput").ap()
    Ws = nc.dram_tensor("Ws", [D, H * M], F32, kind="ExternalInput").ap()
    Wo = nc.dram_tensor("Wo", [H * DV, D], F32, kind="ExternalInput").ap()
    Wgu = nc.dram_tensor("Wgu", [2 * HF, D], F32, kind="ExternalInput").ap()
    Wp = nc.dram_tensor("Wp", [D, HF], F32, kind="ExternalInput").ap()
    gnw_d = nc.dram_tensor("gnw", [128, 1], F32, kind="ExternalInput").ap()
    flag_d = nc.dram_tensor("flagv", [128, 1], F32, kind="ExternalInput").ap()
    triu_d = nc.dram_tensor("triu0", [128, 128], BF16, kind="ExternalInput").ap()
    id_d = nc.dram_tensor("ident", [128, 128], BF16, kind="ExternalInput").ap()
    onc_d = nc.dram_tensor("onescol", [128, 1], BF16, kind="ExternalInput").ap()
    onr_d = nc.dram_tensor("onesrow", [1, 128], BF16, kind="ExternalInput").ap()

    y = nc.dram_tensor("y", [TH, D], F32, kind="ExternalOutput").ap()

    dbgk = "ExternalOutput" if debug else "Internal"
    attn_part = nc.dram_tensor("attn_part", [TH, D], F32, kind=dbgk).ap()
    opre_d = nc.dram_tensor("opre", [H * DV, TH], BF16, kind=dbgk).ap()
    hraw_d = nc.dram_tensor("hraw", [TH, HF], BF16, kind=dbgk).ap()
    wpq_d = nc.dram_tensor("wpq", [D, HF], BF16, kind=dbgk).ap()
    if debug:
        dbg_xn = nc.dram_tensor("dbg_xn", [128, KC * TH], BF16, kind="ExternalOutput").ap()
        dbg_Z = nc.dram_tensor("dbg_Z", [128, KC * 1024], BF16, kind="ExternalOutput").ap()
        dbg_Sh = nc.dram_tensor("dbg_Sh", [1, 1024], F32, kind="ExternalOutput").ap()
        dbg_P = nc.dram_tensor("dbg_P", [128, NTC * 1024], BF16, kind="ExternalOutput").ap()
        dbg_Hk = nc.dram_tensor("dbg_Hk", [64, H * 64], BF16, kind="ExternalOutput").ap()
        dbg_Hv = nc.dram_tensor("dbg_Hv", [64, H * 128], BF16, kind="ExternalOutput").ap()
        dbg_xnp = nc.dram_tensor("dbg_xnp", [128, NTC * D], BF16, kind="ExternalOutput").ap()
        dbg_xnf = nc.dram_tensor("dbg_xnf", [128, KC * 128], BF16, kind="ExternalOutput").ap()
        dbg_ps = nc.dram_tensor("dbg_ps", [128, 512], F32, kind="ExternalOutput").ap()
        dbg_ws = nc.dram_tensor("dbg_ws", [128, KC * 1024], BF16, kind="ExternalOutput").ap()
        dbg_ws2 = nc.dram_tensor("dbg_ws2", [128, KC * 1024], BF16, kind="ExternalOutput").ap()
        dbg_ws3 = nc.dram_tensor("dbg_ws3", [128, KC * 1024], BF16, kind="ExternalOutput").ap()
        dbg_cl = nc.dram_tensor("dbg_cl", [128, 512], F32, kind="ExternalOutput").ap()
        dbg_Pp = nc.dram_tensor("dbg_Pp", [128, NTC * 1024], BF16, kind="ExternalOutput").ap()

    P = Prog()
    SEMS = ["dg", "ds", "dc", "pe", "dv", "ac", "po"]
    for s in SEMS:
        P.cnt[s] = 0

    top = ExitStack()
    attn_scope = ExitStack()
    xn_scope = ExitStack()
    with top:
        sbt = lambda n, sh, dt: top.enter_context(nc.sbuf_tensor(n, sh, dt))
        gnw_sb = sbt("gnw_sb", [128, 1], F32)
        flag_sb = sbt("flag_sb", [128, 1], F32)
        triu_sb = sbt("triu_sb", [128, 128], BF16)
        id_sb = sbt("id_sb", [128, 128], BF16)
        onc_sb = sbt("onc_sb", [128, 1], BF16)
        onr_sb = sbt("onr_sb", [1, 128], BF16)
        ssacc = sbt("ssacc", [128, NTC], F32)
        rstd_ffn = sbt("rstd_ffn", [128, NTC], F32)
        epsln_sb = sbt("epsln_sb", [128, 1], F32)

        for dst, src in [(gnw_sb, gnw_d), (flag_sb, flag_d), (triu_sb, triu_d),
                         (id_sb, id_d), (onc_sb, onc_d), (onr_sb, onr_d)]:
            P.do("sync", lambda e, d=dst, s=src: e.dma_start(out=d[:, :], in_=s), "ds", 16)
        ev_const = ("ds", P.cnt["ds"])
        ev_ss0 = P.do("vector", lambda e: e.memset(ssacc[:, :], 0.0), "dv", 1)
        ev_eps = P.do("vector", lambda e: e.memset(epsln_sb[:, :], EPS_LN), "dv", 1)

        def ln_chunks(scope, x_dram, out_ap_fn, consume_fn=None):
            tag = f"{len(P.q['sync'])}"
            xbuf = [scope.enter_context(nc.sbuf_tensor(f"lnx{i}_{tag}", [128, D], F32)) for i in range(2)]
            junk = scope.enter_context(nc.sbuf_tensor(f"lnj_{tag}", [128, D], BF16))
            st = scope.enter_context(nc.sbuf_tensor(f"lns_{tag}", [128, 6], F32))
            in_evs = [None, None]
            out_evs = [None, None]
            ret = []
            for tc in range(NTC):
                b = tc % 2
                P.wait("sync", in_evs[b])
                ld = P.do("sync", lambda e, b=b, tc=tc: e.dma_start(
                    out=xbuf[b][:, :], in_=x_dram[tc * 128:(tc + 1) * 128, :]), "ds", 16)
                P.wait("vector", ld)
                if _V_LN >= 1:
                    e1 = P.do("vector", lambda e, b=b: e.tensor_reduce(
                        st[:, 0:1], xbuf[b][:, :], AX.X, ALU.add), "dv", 1)
                    e2 = P.do("vector", lambda e, b=b: e.scalar_tensor_tensor(
                        junk[:, :], xbuf[b][:, :], 1.0, xbuf[b][:, :], ALU.mult, ALU.mult,
                        accum_out=st[:, 1:2]), "dv", 1)
                    P.waits("vector", [e1, e2])
                    ea = P.do("vector", lambda e: e.tensor_scalar(st[:, 2:3], st[:, 0:1], 1.0 / D, None, ALU.mult), "dv", 1)
                    P.wait("vector", ea)
                    eb = P.do("vector", lambda e: e.tensor_tensor(st[:, 3:4], st[:, 2:3], st[:, 2:3], ALU.mult), "dv", 1)
                    P.wait("vector", eb)
                    e3 = P.do("vector", lambda e: e.scalar_tensor_tensor(
                        st[:, 4:5], st[:, 1:2], 1.0 / D, st[:, 3:4], ALU.mult, ALU.subtract), "dv", 1)
                    P.wait("scalar", e3)
                    eln = P.do("scalar", lambda e: e.activation(st[:, 5:6], st[:, 4:5], AF.Ln, bias=epsln_sb[:, 0:1]), "ac", 1)
                    P.wait("scalar", eln)
                    e4 = P.do("scalar", lambda e: e.activation(st[:, 5:6], st[:, 5:6], AF.Exp, scale=-0.5), "ac", 1)
                    P.wait("vector", e4)
                P.wait("vector", out_evs[b])
                oap = out_ap_fn(tc)
                if _V_LN <= 1:
                    e5 = P.do("vector", lambda e, b=b, o=oap: e.tensor_scalar(
                        o, xbuf[b][:, :], 0.0, 1.0, ALU.subtract, ALU.mult), "dv", 1)
                else:
                    e5 = P.do("vector", lambda e, b=b, o=oap: e.tensor_scalar(
                        o, xbuf[b][:, :], st[:, 2:3], st[:, 5:6], ALU.subtract, ALU.mult), "dv", 1)
                in_evs[b] = e5
                ret.append(e5)
                cons = consume_fn(tc, e5) if consume_fn else None
                out_evs[b] = cons if cons is not None else e5
            return ret

        def stream_w_cast(w_dram, rowsel, colsel, dst, dst_c0=0):
            r0, r1 = rowsel
            c0, c1 = colsel
            nk = (r1 - r0) // 128
            ncol = c1 - c0
            evs = []
            for k4 in range(0, nk, 4):
                kn = min(4, nk - k4)
                src = w_dram[r0 + k4 * 128: r0 + (k4 + kn) * 128, c0:c1].rearrange(
                    "(k p) c -> p k c", p=128)
                evs.append(P.do("gpsimd", lambda e, s=src, k4=k4, kn=kn: e.dma_start(
                    out=dst[:, k4:k4 + kn, dst_c0:dst_c0 + ncol], in_=s), "dg", 16))
            return evs

        def stream_w_cast2(w_dram, rowsel, colsel, dst, stg, dst_c0=0):
            r0, r1 = rowsel
            c0, c1 = colsel
            nk = (r1 - r0) // 128
            ncol = c1 - c0
            evs = []
            stev = [None, None]
            for i, k4 in enumerate(range(0, nk, 4)):
                kn = min(4, nk - k4)
                bb = i % 2
                src = w_dram[r0 + k4 * 128: r0 + (k4 + kn) * 128, c0:c1].rearrange(
                    "(k p) c -> p k c", p=128)
                P.wait("sync", stev[bb])
                eld = P.do("sync", lambda e, s=src, bb=bb, kn=kn, nc_=ncol: e.dma_start(
                    out=stg[bb][:, 0:kn, 0:nc_], in_=s), "ds", 16)
                P.wait("vector", eld)
                ecv = P.do("vector", lambda e, bb=bb, k4=k4, kn=kn, nc_=ncol: e.tensor_copy(
                    dst[:, k4:k4 + kn, dst_c0:dst_c0 + nc_], stg[bb][:, 0:kn, 0:nc_]), "dv", 1)
                stev[bb] = ecv
                evs.append(ecv)
            return evs

        def rsqrt_chain(src_ap, dst_ap, scale, bias_imm):
            e1 = P.do("vector", lambda e: e.tensor_scalar(
                dst_ap, src_ap, scale, bias_imm, ALU.mult, ALU.add), "dv", 1)
            P.wait("scalar", e1)
            eln = P.do("scalar", lambda e: e.activation(dst_ap, dst_ap, AF.Ln), "ac", 1)
            P.wait("scalar", eln)
            return P.do("scalar", lambda e: e.activation(dst_ap, dst_ap, AF.Exp, scale=-0.5), "ac", 1)

        def quant_block(wb, wq, t1, scr, scb, F, ev_in):
            G = F // 64
            sc32 = scr[:, 0:G]
            inv = scr[:, G:2 * G]
            P.wait("vector", ev_in)
            e1 = P.do("vector", lambda e: e.tensor_reduce(
                sc32.rearrange("p g -> p g ()"),
                wb[:, :].rearrange("p (g k) -> p g k", k=64),
                AX.X, ALU.add, apply_absolute_value=True), "dv", 1)
            P.wait("vector", e1)
            ea = P.do("vector", lambda e: e.tensor_scalar(sc32, sc32, 1.0 / 64, 1e-8, ALU.mult, ALU.max), "dv", 1)
            P.wait("vector", ea)
            eb = P.do("vector", lambda e: e.tensor_copy(scb[:, :], sc32), "dv", 1)
            P.wait("vector", eb)
            ec = P.do("vector", lambda e: e.tensor_copy(sc32, scb[:, :]), "dv", 1)
            P.wait("vector", ec)
            e2 = P.do("vector", lambda e: e.reciprocal(inv, sc32), "dv", 1)
            P.wait("gpsimd", ev_in)
            P.wait("gpsimd", e2)
            e3 = P.do("gpsimd", lambda e: e.tensor_tensor(
                wq[:, :].rearrange("p (g k) -> p g k", k=64),
                wb[:, :].rearrange("p (g k) -> p g k", k=64),
                inv.rearrange("p g -> p g ()").broadcast_to((128, G, 64)),
                ALU.mult), "po", 1)
            P.wait("vector", e3)
            P.do("vector", lambda e: e.tensor_scalar(t1[:, :], wq[:, :], 0.5, None, ALU.is_gt))
            e4 = P.do("vector", lambda e: e.tensor_scalar(wb[:, :], wq[:, :], -0.5, None, ALU.is_lt), "dv", 1)
            P.wait("gpsimd", e4)
            e5 = P.do("gpsimd", lambda e: e.tensor_tensor(t1[:, :], t1[:, :], wb[:, :], ALU.subtract), "po", 1)
            P.waits("vector", [e5, e2])
            e6 = P.do("vector", lambda e: e.tensor_tensor(
                wq[:, :].rearrange("p (g k) -> p g k", k=64),
                t1[:, :].rearrange("p (g k) -> p g k", k=64),
                sc32.rearrange("p g -> p g ()").broadcast_to((128, G, 64)), ALU.mult), "dv", 1)
            return e6

        sba = lambda n, sh, dt: attn_scope.enter_context(nc.sbuf_tensor(n, sh, dt))
        Sh_f = sba("Sh_f", [1, H * M], F32)
        Sh_b = sba("Sh_b", [1, H * M], BF16)
        Hk0_b = sba("Hk0_b", [64, H, 64], BF16)
        Hv0_b = sba("Hv0_b", [64, H, 128], BF16)

        with ExitStack() as s1:
            sb1 = lambda n, sh, dt: s1.enter_context(nc.sbuf_tensor(n, sh, dt))
            Ws_low = sb1("Ws_low", [128, KC, H * M], BF16)
            xnpre_tm = sb1("xnpre_tm", [128, NTC, D], BF16)
            Ppre_tm = sb1("Ppre_tm", [128, NTC, H * M], BF16)

            with ExitStack() as s1a:
                P.wait("vector", ev_const)
                P.wait("scalar", ev_eps)
                if _V_LN < 0:
                    ln_pre = [None] * NTC
                elif _os.environ.get("V_2D", "0") == "1":
                    xnpre_flat = xnpre_tm[:, :, :].rearrange("p k f -> p (k f)")
                    ln_pre = ln_chunks(s1a, x_pre,
                                       lambda tc: xnpre_flat[:, tc * D:(tc + 1) * D])
                else:
                    ln_pre = ln_chunks(s1a, x_pre, lambda tc: xnpre_tm[:, tc, :])
                P.barrier()

            with ExitStack() as s1b:
                sb1b = lambda n, sh, dt: s1b.enter_context(nc.sbuf_tensor(n, sh, dt))
                Ws_sb = Ws_low
                xnf_c = [sb1b(f"xnf{i}", [128, KC, 128], BF16) for i in range(2)]
                sclip = sb1b("sclip", [128, H * M], F32)
                psA = s1b.enter_context(nc.psum_tensor("psA", [128, 512], F32))
                psB = s1b.enter_context(nc.psum_tensor("psB", [128, 512], F32))
                psS0 = s1b.enter_context(nc.psum_tensor("psS0", [1, 512], F32))
                psS1 = s1b.enter_context(nc.psum_tensor("psS1", [1, 512], F32))
                ws_evs = stream_w_cast(Ws, (0, D), (0, H * M), Ws_sb)
                if debug:
                    P.barrier()
                    P.waits("gpsimd", ws_evs)
                    P.waits("vector", ws_evs)
                    for kk in range(16):
                        ecp2 = P.do("vector", lambda e, kk=kk, sc=sclip, W=Ws_sb: e.tensor_copy(
                            sc.bitcast(BF16)[:, 0:1024], W[:, kk, :]), "dv", 1)
                        P.wait("sync", ecp2)
                        edm2 = P.do("sync", lambda e, kk=kk, sc=sclip: e.dma_start(
                            out=dbg_ws2[:, kk * 1024:(kk + 1) * 1024],
                            in_=sc.bitcast(BF16)[:, 0:1024]), "ds", 16)
                        P.wait("vector", edm2)
                spool = Pool(P, [psA, psB])
                last_sh = []
                xnf_free = [None, None]
                eex_prev = [None, None]
                _order = list(range(NTC))[::-1] if _V_REV else list(range(NTC))
                if _V_L < 2:
                    _order = []
                for ti, tc in enumerate(_order):
                    b = ti % 2
                    P.wait("scalar", xnf_free[b])
                    P.wait("scalar", ln_pre[tc])
                    etr = P.do("scalar", lambda e, b=b, tc=tc: e.dma_start_transpose(
                        xnf_c[b][:, :, :],
                        xnpre_tm[:, tc, :].rearrange("p (k f) -> p k f", f=128)), "dc", 16)
                    if debug and tc == 0:
                        P.wait("gpsimd", etr)
                        P.do("gpsimd", lambda e: e.dma_start(
                            out=dbg_xnf, in_=xnf_c[0][:, :, :].rearrange("p k f -> p (k f)")), "dg", 16)
                    if _V_L < 3:
                        xnf_free[b] = etr
                        continue
                    for nh in range(2):
                        slot = spool.acquire("tensor")
                        ps = slot[0]
                        P.wait("tensor", etr)

                        if tc == 0 and nh == 0:
                            P.waits("tensor", ws_evs)
                        for kc in range(KC):
                            sig = "pe" if kc == KC - 1 else None
                            epe = P.do("tensor", lambda e, b=b, kc=kc, nh=nh, ps=ps, W=Ws_sb: e.matmul(
                                ps[:, :], xnf_c[b][:, kc, :], W[:, kc, nh * 512:(nh + 1) * 512],
                                start=(kc == 0), stop=(kc == KC - 1)), sig, 1)
                        P.waits("vector", [epe, eex_prev[nh]])
                        if debug and tc == 0 and nh == 0:
                            epscp = P.do("vector", lambda e, ps=ps: e.tensor_copy(
                                sclip[:, 0:512], ps[:, :]), "dv", 1)
                            P.wait("sync", epscp)
                            edps = P.do("sync", lambda e: e.dma_start(out=dbg_ps, in_=sclip[:, 0:512]), "ds", 16)
                            P.wait("vector", edps)
                        ecl = P.do("vector", lambda e, nh=nh, ps=ps, sc=sclip: e.tensor_scalar(
                            sc[:, nh * 512:(nh + 1) * 512], ps[:, :], -32.0, 32.0,
                            ALU.max, ALU.min), "dv", 1)
                        slot[1] = ecl
                        if debug and tc == 0 and nh == 0:
                            P.wait("sync", ecl)
                            edcl = P.do("sync", lambda e: e.dma_start(out=dbg_cl, in_=sclip[:, 0:512]), "ds", 16)
                            P.wait("vector", edcl)
                            P.wait("scalar", edcl)
                        if _V_L < 5:
                            eex_prev[nh] = ecl
                            last_sh.append(ecl)
                            continue
                        P.wait("scalar", ecl)
                        if _V_CLIP_ONLY:
                            eex = P.do("scalar", lambda e, tc=tc, nh=nh, sc=sclip: e.copy(
                                Ppre_tm[:, tc, nh * 512:(nh + 1) * 512],
                                sc[:, nh * 512:(nh + 1) * 512]), "ac", 1)
                        else:
                            eex = P.do("scalar", lambda e, tc=tc, nh=nh, sc=sclip: e.activation(
                                Ppre_tm[:, tc, nh * 512:(nh + 1) * 512],
                                sc[:, nh * 512:(nh + 1) * 512], AF.Exp), "ac", 1)
                        eex_prev[nh] = eex
                        if _V_L >= 6 and not _V_NO_SH:
                            P.wait("tensor", eex)
                            esh = P.do("tensor", lambda e, tc=tc, nh=nh, ti=ti: e.matmul(
                                (psS0 if nh == 0 else psS1)[:, :], onc_sb[:, :],
                                Ppre_tm[:, tc, nh * 512:(nh + 1) * 512],
                                start=(ti == 0), stop=(ti == NTC - 1)), "pe", 1)
                            if ti == NTC - 1:
                                last_sh.append(esh)
                        else:
                            last_sh.append(eex)
                            Pdummy = 0
                    xnf_free[b] = epe
                P.waits("vector", last_sh)
                if _V_L >= 6:
                    ef0 = P.do("vector", lambda e: e.tensor_scalar(
                        Sh_f[0:1, 0:512], psS0[:, :], flag_sb[0:1, 0:1], None, ALU.mult), "dv", 1)
                    ef1 = P.do("vector", lambda e: e.tensor_scalar(
                        Sh_f[0:1, 512:1024], psS1[:, :], flag_sb[0:1, 0:1], None, ALU.mult), "dv", 1)
                    P.waits("vector", [ef0, ef1])
                else:
                    P.do("vector", lambda e: e.memset(Sh_f[0:1, :], 0.0), "dv", 1)
                eshb = P.do("vector", lambda e: e.tensor_copy(Sh_b[0:1, :], Sh_f[0:1, :]), "dv", 1)
                P.barrier()
                if debug:
                    P.do("sync", lambda e, W=Ws_sb: e.dma_start(
                        out=dbg_ws, in_=W[:, :, :].rearrange("p k f -> p (k f)")), "ds", 16)
                    P.do("sync", lambda e: e.dma_start(
                        out=dbg_xnp, in_=xnpre_tm[:, :, :].rearrange("p k f -> p (k f)")), "ds", 16)
                    P.do("sync", lambda e: e.dma_start(
                        out=dbg_Pp, in_=Ppre_tm[:, :, :].rearrange("p k f -> p (k f)")), "ds", 16)
                    P.barrier()

            with ExitStack() as s1c:
                sb1c = lambda n, sh, dt: s1c.enter_context(nc.sbuf_tensor(n, sh, dt))
                Z_sb = sb1c("Z_sb", [128, KC, H * M], BF16)
                Wh_sb = sb1c("Wh_sb", [128, KC, 192], BF16)
                psA2 = s1c.enter_context(nc.psum_tensor("psA2", [128, 512], F32))
                psB2 = s1c.enter_context(nc.psum_tensor("psB2", [128, 512], F32))
                psH = s1c.enter_context(nc.psum_tensor("psH", [64, 192], F32))
                zpool = Pool(P, [psA2, psB2])
                _V_S0b = _os.environ.get("V_S0", "0") == "1"
                for kc in (range(KC) if not _V_S0b else []):
                    for nh in range(2):
                        slot = zpool.acquire("tensor")
                        ps = slot[0]
                        for tc in range(NTC):
                            sig = "pe" if tc == NTC - 1 else None
                            epe = P.do("tensor", lambda e, kc=kc, nh=nh, tc=tc, ps=ps: e.matmul(
                                ps[:, :], xnpre_tm[:, tc, kc * 128:(kc + 1) * 128],
                                Ppre_tm[:, tc, nh * 512:(nh + 1) * 512],
                                start=(tc == 0), stop=(tc == NTC - 1)), sig, 1)
                        P.wait("vector", epe)
                        ez = P.do("vector", lambda e, kc=kc, nh=nh, ps=ps: e.tensor_scalar(
                            Z_sb[:, kc, nh * 512:(nh + 1) * 512], ps[:, :],
                            flag_sb[:, 0:1], None, ALU.mult), "dv", 1)
                        slot[1] = ez
                ev_z = ("dv", P.cnt["dv"])
                if debug:
                    P.barrier()
                    P.do("sync", lambda e: e.dma_start(
                        out=dbg_Z, in_=Z_sb[:, :, :].rearrange("p k f -> p (k f)")), "ds", 16)
                    P.do("sync", lambda e: e.dma_start(out=dbg_Sh, in_=Sh_f[0:1, :]), "ds", 16)
                ehprev = None
                for h in (range(H) if not _V_S0b else []):
                    evw1 = stream_w_cast(Wk, (0, D), (h * 64, (h + 1) * 64), Wh_sb)
                    evw2 = stream_w_cast(Wv, (0, D), (h * 128, (h + 1) * 128), Wh_sb, dst_c0=64)
                    P.waits("tensor", evw1 + evw2 + [ev_z, ehprev])
                    for kc in range(KC):
                        P.do("tensor", lambda e, h=h, kc=kc: e.matmul(
                            psH[:, 0:64], Wh_sb[:, kc, 0:64], Z_sb[:, kc, h * 64:(h + 1) * 64],
                            start=(kc == 0), stop=(kc == KC - 1)))
                    for kc in range(KC):
                        sig = "pe" if kc == KC - 1 else None
                        eph = P.do("tensor", lambda e, h=h, kc=kc: e.matmul(
                            psH[:, 64:192], Z_sb[:, kc, h * 64:(h + 1) * 64],
                            Wh_sb[:, kc, 64:192],
                            start=(kc == 0), stop=(kc == KC - 1)), sig, 1)
                    P.wait("scalar", eph)
                    P.do("scalar", lambda e, h=h: e.copy(Hk0_b[:, h, :], psH[:, 0:64]))
                    ehprev = P.do("scalar", lambda e, h=h: e.copy(Hv0_b[:, h, :], psH[:, 64:192]), "ac", 1)
                    P.wait("gpsimd", eph)
                P.barrier()
                if debug:
                    P.do("sync", lambda e: e.dma_start(
                        out=dbg_Hk, in_=Hk0_b[:, :, :].rearrange("p k f -> p (k f)")), "ds", 16)
                    P.do("sync", lambda e: e.dma_start(
                        out=dbg_Hv, in_=Hv0_b[:, :, :].rearrange("p k f -> p (k f)")), "ds", 16)
                    P.barrier()

        _V_S0 = _os.environ.get("V_S0", "0") == "1"
        Po_tm = sba("Po_tm", [128, NTC, H * M], BF16)
        xn_fm = xn_scope.enter_context(nc.sbuf_tensor("xn_fm", [128, KC, TH], BF16, side="right"))
        with ExitStack() as s1d:
            if _V_S0:
                pass
            xn_tm_c = [s1d.enter_context(nc.sbuf_tensor(f"xnt{i}", [128, D], BF16)) for i in range(2)]

            def own_consume(tc, ev_out):
                P.wait("scalar", ev_out)
                return P.do("scalar", lambda e, tc=tc: e.dma_start_transpose(
                    xn_fm[:, :, tc * 128:(tc + 1) * 128],
                    xn_tm_c[tc % 2][:, :].rearrange("p (k f) -> p k f", f=128)), "dc", 16)
            if not _V_S0:
                ln_chunks(s1d, x_own, lambda tc: xn_tm_c[tc % 2][:, :], own_consume)
            P.barrier()
        with ExitStack() as s1e:
            sb1e = lambda n, sh, dt: s1e.enter_context(nc.sbuf_tensor(n, sh, dt))
            Ws_ob = sb1e("Ws_sb2", [128, KC, H * M], BF16)
            sclip_o = sb1e("sclip2", [128, H * M], F32)
            psA3 = s1e.enter_context(nc.psum_tensor("psA3", [128, 512], F32))
            psB3 = s1e.enter_context(nc.psum_tensor("psB3", [128, 512], F32))
            ws_evs = stream_w_cast(Ws, (0, D), (0, H * M), Ws_ob)
            opool = Pool(P, [psA3, psB3])
            eex_prev = [None, None]
            for tc in (range(NTC) if not _V_S0 else []):
                for nh in range(2):
                    slot = opool.acquire("tensor")
                    ps = slot[0]
                    if tc == 0 and nh == 0:
                        P.waits("tensor", ws_evs)
                    for kc in range(KC):
                        sig = "pe" if kc == KC - 1 else None
                        epe = P.do("tensor", lambda e, tc=tc, kc=kc, nh=nh, ps=ps: e.matmul(
                            ps[:, :], xn_fm[:, kc, tc * 128:(tc + 1) * 128],
                            Ws_ob[:, kc, nh * 512:(nh + 1) * 512],
                            start=(kc == 0), stop=(kc == KC - 1)), sig, 1)
                    P.waits("vector", [epe, eex_prev[nh]])
                    ecl = P.do("vector", lambda e, nh=nh, ps=ps: e.tensor_scalar(
                        sclip_o[:, nh * 512:(nh + 1) * 512], ps[:, :], -32.0, 32.0,
                        ALU.max, ALU.min), "dv", 1)
                    slot[1] = ecl
                    P.wait("scalar", ecl)
                    eex_prev[nh] = P.do("scalar", lambda e, tc=tc, nh=nh: e.activation(
                        Po_tm[:, tc, nh * 512:(nh + 1) * 512],
                        sclip_o[:, nh * 512:(nh + 1) * 512], AF.Exp), "ac", 1)
            P.barrier()
            if debug:
                P.do("sync", lambda e, W=Ws_ob: e.dma_start(
                    out=dbg_ws3, in_=W[:, :, :].rearrange("p k f -> p (k f)")), "ds", 16)
                P.barrier()
        if debug:
            P.do("sync", lambda e: e.dma_start(
                out=dbg_xn, in_=xn_fm[:, :, :].rearrange("p k f -> p (k f)")), "ds", 16)
            P.do("sync", lambda e: e.dma_start(
                out=dbg_P, in_=Po_tm[:, :, :].rearrange("p k f -> p (k f)")), "ds", 16)
            P.barrier()

        if stage >= 2:
            with ExitStack() as s2:
                sb2 = lambda n, sh, dt: s2.enter_context(nc.sbuf_tensor(n, sh, dt))
                q_fm = sb2("q_fm", [64, HPG, TH], BF16)
                k_fm = sb2("k_fm", [64, HPG, TH], BF16)
                v_tm = sb2("v_tm", [128, NTC, HPG * DV], BF16)
                Wqk_sb = sb2("Wqk_sb", [128, KC, HPG * DK], BF16)
                Wvg_sb = sb2("Wvg_sb", [128, KC, HPG * DV], BF16)
                At_m = sb2("At_m", [128, 2, 256], BF16)
                Gt_m = sb2("Gt_m", [128, 2, 256], BF16)
                invS = sb2("invS", [128, 2, 128], F32)
                okf = sb2("okf", [128, 2, 128], F32)
                red = sb2("red", [128, 2, 8], F32)
                wexp = sb2("wexp", [128, 2, 128], F32)
                wfac = sb2("wfac", [128, 2, 128], F32)
                w_b = sb2("w_b", [128, 2, 128], BF16)
                w_t = sb2("w_t", [64, 2, 256], BF16)
                P_fm = sb2("P_fm", [64, 2, 256], BF16)
                k_tm = sb2("k_tm", [128, 2, 128], BF16)
                Hk_b = sb2("Hk_b", [64, 2, 128], BF16)
                Hv_b = sb2("Hv_b", [64, 2, 256], BF16)
                S_b = sb2("S_b", [1, 2, 128], BF16)
                ovt = sb2("ovt", [128, HPG * DV], F32)
                sg_sb = sb2("sg_sb", [128, HPG * DV], F32)
                grstd = sb2("grstd", [128, 8], F32)
                opre_b = sb2("opre_b", [128, HPG * DV], BF16)
                opre_st = sb2("opre_st", [128, HPG, 128], BF16)
                sqj = sb2("sqj", [128, 128], BF16)
                pst = [s2.enter_context(nc.psum_tensor(f"pst{i}", [64, 512], F32)) for i in range(2)]
                pf32 = [s2.enter_context(nc.psum_tensor(f"pfz{i}", [128, 512], F32)) for i in range(4)]
                pbf = [s2.enter_context(nc.psum_tensor(f"pbz{i}", [128, 128], BF16)) for i in range(2)]
                fpool = Pool(P, pf32)
                bpool = Pool(P, pbf)

                def proj_feat(out_sb, scale, ev_w):
                    epe = None
                    for mt in range(HPG):
                        for ng in range(4):
                            slot = fpool.acquire("tensor")
                            ps = slot[0]
                            if ev_w is not None and mt == 0 and ng == 0:
                                P.waits("tensor", ev_w)
                            for kc in range(KC):
                                sig = "pe" if kc == KC - 1 else None
                                epe = P.do("tensor", lambda e, mt=mt, ng=ng, kc=kc, ps=ps: e.matmul(
                                    ps[0:64, :], Wqk_sb[:, kc, mt * 64:(mt + 1) * 64],
                                    xn_fm[:, kc, ng * 512:(ng + 1) * 512],
                                    start=(kc == 0), stop=(kc == KC - 1)), sig, 1)
                            P.wait("vector", epe)
                            if scale is None:
                                ee = P.do("vector", lambda e, mt=mt, ng=ng, ps=ps, o=out_sb: e.tensor_copy(
                                    o[0:64, mt, ng * 512:(ng + 1) * 512], ps[0:64, :]), "dv", 1)
                            else:
                                ee = P.do("vector", lambda e, mt=mt, ng=ng, ps=ps, o=out_sb: e.tensor_scalar(
                                    o[0:64, mt, ng * 512:(ng + 1) * 512], ps[0:64, :], scale, None,
                                    ALU.mult), "dv", 1)
                            slot[1] = ee
                    return epe

                for hg in range(_V_NHG):
                    c64 = hg * HPG * DK
                    c128 = hg * HPG * DV
                    evq = stream_w_cast(Wq, (0, D), (c64, c64 + HPG * DK), Wqk_sb)
                    eq_last = proj_feat(q_fm, 0.125, evq)
                    P.waits("gpsimd", [eq_last])
                    evk = stream_w_cast(Wk, (0, D), (c64, c64 + HPG * DK), Wqk_sb)
                    proj_feat(k_fm, None, evk)
                    evv = stream_w_cast(Wv, (0, D), (c128, c128 + HPG * DV), Wvg_sb)
                    vlast = None
                    for tc in range(NTC):
                        slot = fpool.acquire("tensor")
                        ps = slot[0]
                        if tc == 0:
                            P.waits("tensor", evv)
                        for kc in range(KC):
                            sig = "pe" if kc == KC - 1 else None
                            epe = P.do("tensor", lambda e, tc=tc, kc=kc, ps=ps: e.matmul(
                                ps[:, :], xn_fm[:, kc, tc * 128:(tc + 1) * 128],
                                Wvg_sb[:, kc, :], start=(kc == 0), stop=(kc == KC - 1)), sig, 1)
                        P.wait("vector", epe)
                        ee = P.do("vector", lambda e, tc=tc, ps=ps: e.tensor_copy(
                            v_tm[:, tc, :], ps[:, :]), "dv", 1)
                        slot[1] = ee
                        vlast = epe
                    P.waits("gpsimd", [vlast])
                    evg = stream_w_cast(Wg, (0, D), (c128, c128 + HPG * DV), Wvg_sb)
                    ev_proj = ("dv", P.cnt["dv"])

                    ev_delta = [None, None]
                    for pr in range(2):
                        pstate = pst[pr]
                        P.waits("tensor", [eshb, ev_proj])
                        for lh in range(2):
                            h = hg * HPG + pr * 2 + lh
                            P.do("tensor", lambda e, lh=lh, h=h, ps=pstate: e.matmul(
                                ps[0:64, lh * 64:(lh + 1) * 64], id_sb[0:64, 0:64],
                                Hk0_b[:, h, :], start=(lh == 0), stop=False))
                            P.do("tensor", lambda e, lh=lh, h=h, ps=pstate: e.matmul(
                                ps[0:64, 128 + lh * 128:128 + (lh + 1) * 128], id_sb[0:64, 0:64],
                                Hv0_b[:, h, :], start=False, stop=False))
                        mofs = (hg * HPG + pr * 2) * M
                        ev_delta[pr] = P.do("tensor", lambda e, ps=pstate, mo=mofs: e.matmul(
                            ps[0:1, 384:512], onr_sb[0:1, 0:1], Sh_b[0:1, mo:mo + 128],
                            start=False, stop=False), "pe", 1)

                    ovev = None
                    ewf_prev = [None, None]
                    for c in range(_V_NCH):
                        eove2 = None
                        for pr in range(2):
                            pstate = pst[pr]
                            mofs = (hg * HPG + pr * 2) * M
                            vofs = pr * 2 * DV
                            P.wait("scalar", ev_delta[pr])
                            P.do("scalar", lambda e, pr=pr, ps=pstate: e.copy(
                                Hk_b[0:64, pr, :], ps[0:64, 0:128]))
                            P.do("scalar", lambda e, pr=pr, ps=pstate: e.copy(
                                Hv_b[0:64, pr, :], ps[0:64, 128:384]))
                            ecp = P.do("scalar", lambda e, pr=pr, ps=pstate: e.copy(
                                S_b[0:1, pr, :], ps[0:1, 384:512]), "ac", 1)
                            ek = ep = None
                            for lh in range(2):
                                hloc = pr * 2 + lh
                                bs1 = bpool.acquire("tensor")
                                et1 = P.do("tensor", lambda e, c=c, hl=hloc, b=bs1[0]: e.transpose(
                                    b[:, 0:64], k_fm[0:64, hl, c * 128:(c + 1) * 128],
                                    id_sb[0:64, 0:64]), "pe", 1)
                                P.wait("scalar", et1)
                                ek = P.do("scalar", lambda e, pr=pr, lh=lh, b=bs1[0]: e.copy(
                                    k_tm[:, pr, lh * 64:(lh + 1) * 64], b[:, 0:64]), "ac", 1)
                                bs1[1] = ek
                                bs2 = bpool.acquire("tensor")
                                et2 = P.do("tensor", lambda e, c=c, mo=mofs, lh=lh, b=bs2[0]: e.transpose(
                                    b[0:64, :], Po_tm[:, c, mo + lh * 64:mo + (lh + 1) * 64],
                                    id_sb[:, :]), "pe", 1)
                                P.wait("scalar", et2)
                                ep = P.do("scalar", lambda e, pr=pr, lh=lh, b=bs2[0]: e.copy(
                                    P_fm[0:64, pr, lh * 128:(lh + 1) * 128], b[0:64, :]), "ac", 1)
                                bs2[1] = ep
                            if _V_SC < 2:
                                continue
                            fsA = fpool.acquire("tensor")
                            for lh in range(2):
                                hloc = pr * 2 + lh
                                sig = "pe" if lh == 1 else None
                                eat = P.do("tensor", lambda e, c=c, hl=hloc, lh=lh, ps=fsA[0]: e.matmul(
                                    ps[:, lh * 128:(lh + 1) * 128],
                                    k_fm[0:64, hl, c * 128:(c + 1) * 128],
                                    q_fm[0:64, hl, c * 128:(c + 1) * 128],
                                    start=True, stop=True), sig, 1)
                            P.wait("vector", eat)
                            for lh in range(2):
                                sig = "dv" if lh == 1 else None
                                eam = P.do("vector", lambda e, pr=pr, lh=lh, ps=fsA[0]: e.tensor_tensor(
                                    At_m[:, pr, lh * 128:(lh + 1) * 128],
                                    ps[:, lh * 128:(lh + 1) * 128], triu_sb[:, :], ALU.mult), sig, 1)
                            fsA[1] = eam
                            if _V_SC < 3:
                                continue
                            fsS = fpool.acquire("tensor")
                            P.wait("tensor", ecp)
                            P.do("tensor", lambda e, pr=pr, ps=fsS[0]: e.matmul(
                                ps[:, 0:128], onr_sb[0:1, :], S_b[0:1, pr, :],
                                start=True, stop=False))
                            es = P.do("tensor", lambda e, c=c, mo=mofs, ps=fsS[0]: e.matmul(
                                ps[:, 0:128], triu_sb[:, :], Po_tm[:, c, mo:mo + 128],
                                start=False, stop=True), "pe", 1)
                            P.wait("scalar", es)
                            P.wait("scalar", ewf_prev[pr])
                            elnS = P.do("scalar", lambda e, pr=pr, ps=fsS[0]: e.activation(
                                invS[:, pr, :], ps[:, 0:128], AF.Ln), "ac", 1)
                            P.wait("scalar", elnS)
                            einv = P.do("scalar", lambda e, pr=pr: e.activation(
                                invS[:, pr, :], invS[:, pr, :], AF.Exp, scale=-1.0), "ac", 1)
                            fsS[1] = einv
                            if _V_SC < 4:
                                continue
                            fsO = fpool.acquire("tensor")
                            P.wait("tensor", eam)
                            for lh in range(2):
                                hloc = pr * 2 + lh
                                P.do("tensor", lambda e, c=c, pr=pr, hl=hloc, lh=lh, ps=fsO[0]: e.matmul(
                                    ps[:, lh * 64:(lh + 1) * 64],
                                    q_fm[0:64, hl, c * 128:(c + 1) * 128],
                                    Hk_b[0:64, pr, lh * 64:(lh + 1) * 64],
                                    start=True, stop=False))
                                sig = "pe" if lh == 1 else None
                                eok = P.do("tensor", lambda e, c=c, pr=pr, lh=lh, mo=mofs, ps=fsO[0]: e.matmul(
                                    ps[:, lh * 64:(lh + 1) * 64],
                                    At_m[:, pr, lh * 128:(lh + 1) * 128],
                                    Po_tm[:, c, mo + lh * 64:mo + (lh + 1) * 64],
                                    start=False, stop=True), sig, 1)
                            P.waits("vector", [eok, einv])
                            emu = P.do("vector", lambda e, pr=pr, ps=fsO[0]: e.tensor_tensor(
                                okf[:, pr, :], ps[:, 0:128], invS[:, pr, :], ALU.mult), "dv", 1)
                            fsO[1] = emu
                            erm = P.do("vector", lambda e, pr=pr: e.tensor_reduce(
                                red[:, pr, 0:2].rearrange("p f -> p f ()"),
                                okf[:, pr, :].rearrange("p (h m) -> p h m", m=64),
                                AX.X, ALU.max, negate=True), "dv", 1)
                            P.wait("vector", erm)
                            esub = P.do("vector", lambda e, pr=pr: e.tensor_tensor(
                                okf[:, pr, :].rearrange("p (h m) -> p h m", m=64),
                                okf[:, pr, :].rearrange("p (h m) -> p h m", m=64),
                                red[:, pr, 0:2].rearrange("p f -> p f ()").broadcast_to((128, 2, 64)),
                                ALU.add), "dv", 1)
                            P.wait("scalar", esub)
                            eex = P.do("scalar", lambda e, pr=pr: e.activation(
                                wexp[:, pr, :], okf[:, pr, :], AF.Exp), "ac", 1)
                            P.wait("vector", eex)
                            ers = P.do("vector", lambda e, pr=pr: e.tensor_reduce(
                                red[:, pr, 2:4].rearrange("p f -> p f ()"),
                                wexp[:, pr, :].rearrange("p (h m) -> p h m", m=64),
                                AX.X, ALU.add), "dv", 1)
                            P.wait("vector", ers)
                            erc = P.do("vector", lambda e, pr=pr: e.reciprocal(
                                red[:, pr, 4:6], red[:, pr, 2:4]), "dv", 1)
                            P.wait("vector", erc)
                            ewf = P.do("vector", lambda e, pr=pr: e.tensor_tensor(
                                wfac[:, pr, :].rearrange("p (h m) -> p h m", m=64),
                                invS[:, pr, :].rearrange("p (h m) -> p h m", m=64),
                                red[:, pr, 4:6].rearrange("p f -> p f ()").broadcast_to((128, 2, 64)),
                                ALU.mult), "dv", 1)
                            P.wait("vector", ewf)
                            ewb = P.do("vector", lambda e, pr=pr: e.tensor_tensor(
                                w_b[:, pr, :], wexp[:, pr, :], wfac[:, pr, :], ALU.mult), "dv", 1)
                            ewf_prev[pr] = ewf
                            if _V_SC < 5:
                                continue
                            ewt = None
                            for lh in range(2):
                                bs3 = bpool.acquire("tensor")
                                P.wait("tensor", ewb)
                                et3 = P.do("tensor", lambda e, pr=pr, lh=lh, b=bs3[0]: e.transpose(
                                    b[0:64, :], w_b[:, pr, lh * 64:(lh + 1) * 64],
                                    id_sb[:, :]), "pe", 1)
                                P.wait("scalar", et3)
                                ewt = P.do("scalar", lambda e, pr=pr, lh=lh, b=bs3[0]: e.copy(
                                    w_t[0:64, pr, lh * 128:(lh + 1) * 128], b[0:64, :]), "ac", 1)
                                bs3[1] = ewt
                            if _V_SC < 5:
                                continue
                            fsG = fpool.acquire("tensor")
                            P.waits("tensor", [ewt, ep])
                            for lh in range(2):
                                sig = "pe" if lh == 1 else None
                                eg = P.do("tensor", lambda e, pr=pr, lh=lh, ps=fsG[0]: e.matmul(
                                    ps[:, lh * 128:(lh + 1) * 128],
                                    P_fm[0:64, pr, lh * 128:(lh + 1) * 128],
                                    w_t[0:64, pr, lh * 128:(lh + 1) * 128],
                                    start=True, stop=True), sig, 1)
                            P.wait("vector", eg)
                            for lh in range(2):
                                sig = "dv" if lh == 1 else None
                                egm = P.do("vector", lambda e, pr=pr, lh=lh, ps=fsG[0]: e.tensor_tensor(
                                    Gt_m[:, pr, lh * 128:(lh + 1) * 128],
                                    ps[:, lh * 128:(lh + 1) * 128], triu_sb[:, :], ALU.mult), sig, 1)
                            fsG[1] = egm
                            if _V_SC < 6:
                                continue
                            fsV = fpool.acquire("tensor")
                            P.wait("tensor", egm)
                            for lh in range(2):
                                P.do("tensor", lambda e, pr=pr, lh=lh, ps=fsV[0]: e.matmul(
                                    ps[:, lh * 128:(lh + 1) * 128],
                                    w_t[0:64, pr, lh * 128:(lh + 1) * 128],
                                    Hv_b[0:64, pr, lh * 128:(lh + 1) * 128],
                                    start=True, stop=False))
                                sig = "pe" if lh == 1 else None
                                eov = P.do("tensor", lambda e, c=c, pr=pr, lh=lh, vo=vofs, ps=fsV[0]: e.matmul(
                                    ps[:, lh * 128:(lh + 1) * 128],
                                    Gt_m[:, pr, lh * 128:(lh + 1) * 128],
                                    v_tm[:, c, vo + lh * DV:vo + (lh + 1) * DV],
                                    start=False, stop=True), sig, 1)
                            P.wait("vector", eov)
                            if pr == 0:
                                P.wait("vector", ovev)
                            eove = P.do("vector", lambda e, pr=pr, vo=vofs, ps=fsV[0]: e.tensor_copy(
                                ovt[:, vo:vo + 256], ps[:, 0:256]), "dv", 1)
                            fsV[1] = eove
                            if pr == 1:
                                eove2 = eove
                            if _V_SC < 7:
                                continue
                            P.wait("tensor", ek)
                            last = (c == _V_NCH - 1)
                            for lh in range(2):
                                P.do("tensor", lambda e, c=c, pr=pr, lh=lh, mo=mofs, ps=pstate: e.matmul(
                                    ps[0:64, lh * 64:(lh + 1) * 64],
                                    k_tm[:, pr, lh * 64:(lh + 1) * 64],
                                    Po_tm[:, c, mo + lh * 64:mo + (lh + 1) * 64],
                                    start=False, stop=last))
                                P.do("tensor", lambda e, c=c, pr=pr, lh=lh, mo=mofs, vo=vofs, ps=pstate: e.matmul(
                                    ps[0:64, 128 + lh * 128:128 + (lh + 1) * 128],
                                    Po_tm[:, c, mo + lh * 64:mo + (lh + 1) * 64],
                                    v_tm[:, c, vo + lh * DV:vo + (lh + 1) * DV],
                                    start=False, stop=last))
                            ev_delta[pr] = P.do("tensor", lambda e, c=c, mo=mofs, ps=pstate: e.matmul(
                                ps[0:1, 384:512], onc_sb[:, :], Po_tm[:, c, mo:mo + 128],
                                start=False, stop=last), "pe", 1)

                        if not _V_GT:
                            continue
                        slot = fpool.acquire("tensor")
                        ps = slot[0]
                        if c == 0:
                            P.waits("tensor", evg)
                        for kc in range(KC):
                            sig = "pe" if kc == KC - 1 else None
                            epe = P.do("tensor", lambda e, c=c, kc=kc, ps=ps: e.matmul(
                                ps[:, :], xn_fm[:, kc, c * 128:(c + 1) * 128],
                                Wvg_sb[:, kc, :], start=(kc == 0), stop=(kc == KC - 1)), sig, 1)
                        P.wait("scalar", epe)
                        esg = P.do("scalar", lambda e, ps=ps: e.activation(
                            sg_sb[:, :], ps[:, :], AF.Silu), "ac", 1)
                        slot[1] = esg
                        P.wait("vector", eove2)
                        sqevs = []
                        for lh in range(HPG):
                            sqevs.append(P.do("vector", lambda e, lh=lh: e.scalar_tensor_tensor(
                                sqj[:, :], ovt[:, lh * DV:(lh + 1) * DV], 1.0,
                                ovt[:, lh * DV:(lh + 1) * DV], ALU.mult, ALU.mult,
                                accum_out=grstd[:, lh:lh + 1]), "dv", 1))
                        P.waits("vector", sqevs)
                        efin = rsqrt_chain(grstd[:, 0:4], grstd[:, 4:8], 1.0 / DV, EPS_LN)
                        P.waits("vector", [efin, esg])
                        egm2 = P.do("vector", lambda e: e.tensor_tensor(
                            ovt[:, :], ovt[:, :], sg_sb[:, :], ALU.mult), "dv", 1)
                        P.wait("vector", egm2)
                        ecb = P.do("vector", lambda e: e.tensor_tensor(
                            opre_b[:, :].rearrange("p (h v) -> p h v", v=DV),
                            ovt[:, :].rearrange("p (h v) -> p h v", v=DV),
                            grstd[:, 4:8].rearrange("p f -> p f ()").broadcast_to((128, 4, DV)),
                            ALU.mult), "dv", 1)
                        ovev = ecb
                        P.wait("scalar", ecb)
                        ett = P.do("scalar", lambda e: e.dma_start_transpose(
                            opre_st[:, :, :],
                            opre_b[:, :].rearrange("p (j f) -> p j f", f=128)), "dc", 16)
                        P.wait("sync", ett)
                        est = P.do("sync", lambda e, hg=hg, c=c: e.dma_start(
                            out=opre_d[hg * 512:(hg + 1) * 512, c * 128:(c + 1) * 128].rearrange(
                                "(J p) t -> p J t", p=128),
                            in_=opre_st[:, :, :]), "ds", 16)
                        P.wait("scalar", est)
                        P.wait("vector", est)
                    P.barrier()

        attn_scope.close()

        if stage >= 3:
            with ExitStack() as s3:
                sb3 = lambda n, sh, dt: s3.enter_context(nc.sbuf_tensor(n, sh, dt))
                opre_sb = sb3("opre_sb", [128, 16, TH], BF16)
                Wo_sb = sb3("Wo_sb", [128, 16, 512], BF16)
                xsl = [sb3(f"xsl{i}", [128, 512], F32) for i in range(2)]
                osl = [sb3(f"osl{i}", [128, 512], F32) for i in range(2)]
                pf3 = [s3.enter_context(nc.psum_tensor(f"pf3{i}", [128, 512], F32)) for i in range(4)]
                fpool3 = Pool(P, pf3)
                xev = [None, None]
                oev = [None, None]
                lprev = wprev = None
                for dg in range(4):
                    lev = []
                    for vf in range(16):
                        P.wait("sync", lprev)
                        lev.append(P.do("sync", lambda e, vf=vf: e.dma_start(
                            out=opre_sb[:, vf, :], in_=opre_d[vf * 128:(vf + 1) * 128, :]), "ds", 16))
                    P.waits("gpsimd", [wprev])
                    wev = stream_w_cast(Wo, (0, H * DV), (dg * 512, (dg + 1) * 512), Wo_sb)
                    P.waits("vector", wev)
                    gevs = []
                    for vf in range(16):
                        gevs.append(P.do("vector", lambda e, vf=vf: e.tensor_scalar(
                            Wo_sb[:, vf, :], Wo_sb[:, vf, :], gnw_sb[:, 0:1], None, ALU.mult), "dv", 1))
                    P.waits("tensor", gevs + lev)
                    for tc in range(NTC):
                        b = tc % 2
                        P.wait("sync", xev[b])
                        exl = P.do("sync", lambda e, b=b, tc=tc, dg=dg: e.dma_start(
                            out=xsl[b][:, :],
                            in_=x_own[tc * 128:(tc + 1) * 128, dg * 512:(dg + 1) * 512]), "ds", 16)
                        slot = fpool3.acquire("tensor")
                        ps = slot[0]
                        for vf in range(16):
                            sig = "pe" if vf == 15 else None
                            epe = P.do("tensor", lambda e, tc=tc, vf=vf, ps=ps: e.matmul(
                                ps[:, :], opre_sb[:, vf, tc * 128:(tc + 1) * 128],
                                Wo_sb[:, vf, :], start=(vf == 0), stop=(vf == 15)), sig, 1)
                        P.waits("vector", [epe, exl, oev[b]])
                        eadd = P.do("vector", lambda e, b=b, ps=ps: e.tensor_tensor(
                            osl[b][:, :], ps[:, :], xsl[b][:, :], ALU.add), "dv", 1)
                        slot[1] = eadd
                        xev[b] = eadd
                        P.wait("sync", eadd)
                        oev[b] = P.do("sync", lambda e, b=b, tc=tc, dg=dg: e.dma_start(
                            out=attn_part[tc * 128:(tc + 1) * 128, dg * 512:(dg + 1) * 512],
                            in_=osl[b][:, :]), "ds", 16)
                    lprev = oev[0]
                    wprev = epe
                P.barrier()

        if stage >= 4:
            with ExitStack() as s4:
                sb4 = lambda n, sh, dt: s4.enter_context(nc.sbuf_tensor(n, sh, dt))
                WgT = [sb4(f"WgT{i}", [128, KC, 512], BF16) for i in range(2)]
                WuT = [sb4(f"WuT{i}", [128, KC, 512], BF16) for i in range(2)]
                wb_b = [sb4(f"wbb{i}", [128, D], BF16) for i in range(2)]
                wq_b = [sb4(f"wqb{i}", [128, D], BF16) for i in range(2)]
                t1_b = [sb4(f"t1b{i}", [128, D], BF16) for i in range(2)]
                scr4 = sb4("scr4", [128, 2 * 32], F32)
                scb4 = sb4("scb4", [128, 32], BF16)
                silu_t = sb4("silu_t", [128, 512], F32)
                hraw_t = [sb4(f"hrawt{i}", [128, 512], BF16) for i in range(2)]
                sqj4 = sb4("sqj4", [128, 512], BF16)
                sst = sb4("sst", [128, 1], F32)
                pg4 = [s4.enter_context(nc.psum_tensor(f"pg4{i}", [128, 512], F32)) for i in range(4)]
                gpool = Pool(P, pg4)
                qev = [None, None]
                wt_last = [None, None]

                def quant_to(dst_wt, col_jb, w_row0, par):
                    P.wait("gpsimd", qev[par])
                    eld = P.do("gpsimd", lambda e, r0=w_row0, par=par: e.dma_start(
                        out=wb_b[par][:, :], in_=Wgu[r0:r0 + 128, :]), "dg", 16)
                    eq = quant_block(wb_b[par], wq_b[par], t1_b[par], scr4, scb4, D, eld)
                    P.wait("scalar", eq)
                    etr = P.do("scalar", lambda e, par=par, d=dst_wt, jb=col_jb: e.dma_start_transpose(
                        d[:, :, jb * 128:(jb + 1) * 128],
                        wq_b[par][:, :].rearrange("p (k f) -> p k f", f=128)), "dc", 16)
                    qev[par] = etr
                    return etr

                for hgf in range(NHGF):
                    pp = hgf % 2
                    P.waits("scalar", [wt_last[pp]])
                    tevs = []
                    for jb in range(4):
                        tevs.append(quant_to(WgT[pp], jb, hgf * 512 + jb * 128, jb % 2))
                    for jb in range(4):
                        tevs.append(quant_to(WuT[pp], jb, HF + hgf * 512 + jb * 128, jb % 2))
                    P.waits("tensor", tevs)
                    hrev = [None, None]
                    ehr_last = [None]
                    for tc in range(NTC):
                        b = tc % 2
                        sg = gpool.acquire("tensor")
                        su = gpool.acquire("tensor")
                        for kc in range(KC):
                            P.do("tensor", lambda e, tc=tc, kc=kc, pp=pp, ps=sg[0]: e.matmul(
                                ps[:, :], xn_fm[:, kc, tc * 128:(tc + 1) * 128],
                                WgT[pp][:, kc, :], start=(kc == 0), stop=(kc == KC - 1)))
                        for kc in range(KC):
                            sig = "pe" if kc == KC - 1 else None
                            epe = P.do("tensor", lambda e, tc=tc, kc=kc, pp=pp, ps=su[0]: e.matmul(
                                ps[:, :], xn_fm[:, kc, tc * 128:(tc + 1) * 128],
                                WuT[pp][:, kc, :], start=(kc == 0), stop=(kc == KC - 1)), sig, 1)
                        wt_last[pp] = epe
                        P.wait("scalar", epe)
                        P.wait("scalar", ehr_last[0])
                        esi = P.do("scalar", lambda e, ps=sg[0]: e.activation(
                            silu_t[:, :], ps[:, :], AF.Silu), "ac", 1)
                        P.waits("vector", [esi, hrev[b]])
                        ehr = P.do("vector", lambda e, b=b, ps=su[0]: e.scalar_tensor_tensor(
                            hraw_t[b][:, :], ps[:, :], 1.0, silu_t[:, :], ALU.mult, ALU.mult), "dv", 1)
                        sg[1] = ehr
                        su[1] = ehr
                        ehr_last[0] = ehr
                        P.wait("vector", ehr)
                        ess = P.do("vector", lambda e, b=b: e.scalar_tensor_tensor(
                            sqj4[:, :], hraw_t[b][:, :], 1.0, hraw_t[b][:, :], ALU.mult, ALU.mult,
                            accum_out=sst[:, 0:1]), "dv", 1)
                        if hgf == 0 and tc == 0:
                            P.wait("vector", ev_ss0)
                        P.wait("vector", ess)
                        P.do("vector", lambda e, tc=tc: e.tensor_tensor(
                            ssacc[:, tc:tc + 1], ssacc[:, tc:tc + 1], sst[:, 0:1], ALU.add), "dv", 1)
                        P.wait("sync", ehr)
                        hrev[b] = P.do("sync", lambda e, b=b, tc=tc, hgf=hgf: e.dma_start(
                            out=hraw_d[tc * 128:(tc + 1) * 128, hgf * 512:(hgf + 1) * 512],
                            in_=hraw_t[b][:, :]), "ds", 16)
                P.barrier()

        xn_scope.close()

        if stage >= 5:
            er = rsqrt_chain(ssacc[:, :], rstd_ffn[:, :], 1.0 / HF, EPS_RMS)
            wpq_evs = []
            with ExitStack() as s5a:
                sb5a = lambda n, sh, dt: s5a.enter_context(nc.sbuf_tensor(n, sh, dt))
                wb5 = [sb5a(f"wb5{i}", [128, HF], BF16) for i in range(2)]
                wq5 = [sb5a(f"wq5{i}", [128, HF], BF16) for i in range(2)]
                t15 = [sb5a(f"t15{i}", [128, HF], BF16) for i in range(2)]
                scr5 = sb5a("scr5", [128, 2 * 128], F32)
                scb5 = sb5a("scb5", [128, 128], BF16)
                q5ev = [None, None]
                for rb in range(16):
                    par = rb % 2
                    P.wait("gpsimd", q5ev[par])
                    eld = P.do("gpsimd", lambda e, rb=rb, par=par: e.dma_start(
                        out=wb5[par][:, :], in_=Wp[rb * 128:(rb + 1) * 128, :]), "dg", 16)
                    eq = quant_block(wb5[par], wq5[par], t15[par], scr5, scb5, HF, eld)
                    P.wait("sync", eq)
                    est = P.do("sync", lambda e, rb=rb, par=par: e.dma_start(
                        out=wpq_d[rb * 128:(rb + 1) * 128, :], in_=wq5[par][:, :]), "ds", 16)
                    q5ev[par] = est
                    wpq_evs.append(est)
                P.barrier()
            with ExitStack() as s5:
                sb5 = lambda n, sh, dt: s5.enter_context(nc.sbuf_tensor(n, sh, dt))
                WpT = sb5("WpT", [128, 64, 512], BF16)
                hrT = [sb5(f"hrT{i}", [128, 64, 128], BF16) for i in range(2)]
                apsl = [sb5(f"apsl{i}", [128, 512], F32) for i in range(2)]
                ysl = [sb5(f"ysl{i}", [128, 512], F32) for i in range(2)]
                tsl = [sb5(f"tsl{i}", [128, 512], F32) for i in range(2)]
                pf5 = [s5.enter_context(nc.psum_tensor(f"pf5{i}", [128, 512], F32)) for i in range(4)]
                fpool5 = Pool(P, pf5)
                hrev5 = [None, None]
                aev5 = [None, None]
                yev5 = [None, None]
                wptev = None
                for dg in range(4):
                    P.waits("scalar", wpq_evs + [wptev])
                    wpt_evs = []
                    for do4 in range(4):
                        wpt_evs.append(P.do("scalar", lambda e, dg=dg, do4=do4: e.dma_start_transpose(
                            WpT[:, :, do4 * 128:(do4 + 1) * 128],
                            wpq_d[dg * 512 + do4 * 128:dg * 512 + (do4 + 1) * 128, :].rearrange(
                                "p (k f) -> p k f", f=128)), "dc", 16))
                    P.waits("tensor", wpt_evs)
                    for tc in range(NTC):
                        b = tc % 2
                        P.wait("scalar", hrev5[b])
                        ehl = P.do("scalar", lambda e, b=b, tc=tc: e.dma_start_transpose(
                            hrT[b][:, :, :],
                            hraw_d[tc * 128:(tc + 1) * 128, :].rearrange("p (k f) -> p k f", f=128)),
                            "dc", 16)
                        P.wait("sync", aev5[b])
                        eal = P.do("sync", lambda e, b=b, tc=tc, dg=dg: e.dma_start(
                            out=apsl[b][:, :],
                            in_=attn_part[tc * 128:(tc + 1) * 128, dg * 512:(dg + 1) * 512]), "ds", 16)
                        slot = fpool5.acquire("tensor")
                        ps = slot[0]
                        P.wait("tensor", ehl)
                        for hc in range(64):
                            sig = "pe" if hc == 63 else None
                            epe = P.do("tensor", lambda e, b=b, hc=hc, ps=ps: e.matmul(
                                ps[:, :], hrT[b][:, hc, :], WpT[:, hc, :],
                                start=(hc == 0), stop=(hc == 63)), sig, 1)
                        hrev5[b] = epe
                        if dg == 0 and tc == 0:
                            P.wait("vector", er)
                        P.waits("vector", [epe, eal, yev5[b]])
                        emul = P.do("vector", lambda e, b=b, tc=tc, ps=ps: e.tensor_scalar(
                            tsl[b][:, :], ps[:, :], rstd_ffn[:, tc:tc + 1], None, ALU.mult), "dv", 1)
                        slot[1] = emul
                        P.wait("vector", emul)
                        eadd = P.do("vector", lambda e, b=b: e.tensor_tensor(
                            ysl[b][:, :], tsl[b][:, :], apsl[b][:, :], ALU.add), "dv", 1)
                        aev5[b] = eadd
                        P.wait("sync", eadd)
                        yev5[b] = P.do("sync", lambda e, b=b, tc=tc, dg=dg: e.dma_start(
                            out=y[tc * 128:(tc + 1) * 128, dg * 512:(dg + 1) * 512],
                            in_=ysl[b][:, :]), "ds", 16)
                    wptev = epe
        else:
            with ExitStack() as sz:
                zb = sz.enter_context(nc.sbuf_tensor("zb", [128, D], F32))
                P.barrier()
                P.do("vector", lambda e: e.memset(zb[:, :], 0.0), "dv", 1)
                P.barrier()
                for tc in range(NTC):
                    P.do("sync", lambda e, tc=tc: e.dma_start(
                        out=y[tc * 128:(tc + 1) * 128, :], in_=zb[:, :]), "ds", 16)

        P.barrier()
        with nc.Block() as block, ExitStack() as semstack:
            sems = {s: semstack.enter_context(nc.semaphore(s)) for s in SEMS}
            P.play(block, sems)

    return nc


def _stage_inputs(inputs):
    x = inputs["L0_attn_hidden_states"]
    gnw = np.ascontiguousarray(
        np.asarray(inputs["L0_attn_g_norm_weight"], np.float32).reshape(128, 1))
    triu0 = np.triu(np.ones((128, 128), np.float32)).astype(BF)
    ident = np.eye(128, dtype=np.float32).astype(BF)
    onescol = np.ones((128, 1), np.float32).astype(BF)
    onesrow = np.ones((1, 128), np.float32).astype(BF)
    in_maps = []
    for c in range(8):
        b, half = c // 2, c % 2
        im = {
            "x_own": np.ascontiguousarray(x[b, half * TH:(half + 1) * TH]),
            "x_pre": np.ascontiguousarray(x[b, 0:TH]),
            "Wq": inputs["L0_attn_Wq"], "Wk": inputs["L0_attn_Wk"],
            "Wv": inputs["L0_attn_Wv"], "Wg": inputs["L0_attn_Wg"],
            "Ws": inputs["L0_attn_Ws"], "Wo": inputs["L0_attn_Wo"],
            "Wgu": inputs["L0_ffn_w_gate_up"], "Wp": inputs["L0_ffn_w_proj"],
            "gnw": gnw,
            "flagv": np.full((128, 1), float(half), np.float32),
            "triu0": triu0, "ident": ident,
            "onescol": onescol, "onesrow": onesrow,
        }
        in_maps.append(im)
    return in_maps


def kernel(**inputs):
    in_maps = _stage_inputs(inputs)
    nc = build(debug=DEBUG, stage=STAGE)
    res = run_bass_kernel_spmd(nc, in_maps, core_ids=list(range(8)))
    out = np.empty((4, 4096, 2048), np.float32)
    for c in range(8):
        b, half = c // 2, c % 2
        out[b, half * TH:(half + 1) * TH] = np.asarray(res.results[c]["y"])
    return out
